# version 01
# baseline (speedup 1.0000x reference)
import numpy as np
import ml_dtypes
from contextlib import ExitStack
import concourse.bass as bass
import concourse.mybir as mybir
from concourse.bass_utils import run_bass_kernel_spmd
from concourse.aot_env import allow_ici_all_to_all

F32 = mybir.dt.float32
BF16 = mybir.dt.bfloat16
AF = mybir.ActivationFunctionType
ALU = mybir.AluOpType
AX = mybir.AxisListType
NC = 8
RGRP = [list(range(NC))]
P = 128


class Cfg:
    def __init__(self, D=2048, B=4, L=4096, T=32, F=8192, RT_CH=2):
        self.D, self.B, self.L, self.T, self.F = D, B, L, T, F
        self.C = 128
        assert L == T * self.C
        self.R = B * self.C
        self.MC, self.DJ, self.FC = D // NC, D // NC, F // NC
        self.KD, self.KM, self.KF = D // P, D // P, F // P
        self.RG = self.R // P
        assert self.RG == B
        self.MG, self.DG, self.FG = self.MC // P, self.DJ // P, self.FC // P
        self.NS_M = max(1, D // 512)
        self.RT_CH = RT_CH
        self.RT = RT_CH * self.R
        assert T % RT_CH == 0
        self.NT = T // RT_CH
        self.NS_R = max(1, self.RT // 512)
        assert self.R <= 512


class Sem:
    def __init__(self, stack, nc, name):
        self.h = stack.enter_context(nc.semaphore(name))
        self.n = 0


class Buf:
    def __init__(self):
        self.ready = None
        self.uses = []

    def wguard(self):
        w = list(self.uses)
        if self.ready is not None:
            w.append(self.ready)
        return w

    def rguard(self):
        return [self.ready] if self.ready is not None else []

    def wrote(self, tag):
        self.ready = tag
        self.uses = []

    def used(self, tag):
        self.uses.append(tag)


class Prog:
    ENGS = ("tensor", "vector", "scalar", "sync", "gpsimd")

    def __init__(self, nc):
        self.nc = nc
        self.q = {e: [] for e in self.ENGS}

    def op(self, eng, builder, wait=(), inc=()):
        waits = []
        seen = {}
        for (s, t) in wait:
            t = int(t)
            if t <= 0:
                continue
            if s.h not in seen or seen[s.h][1] < t:
                seen[s.h] = (s.h, t)
        waits = list(seen.values())
        incs = []
        for s, k in inc:
            s.n += k
            incs.append((s.h, k))

        def fn(e):
            for h, t in waits:
                e.wait_ge(h, t)
            inst = builder(e)
            for h, k in incs:
                inst.then_inc(h, k)
        self.q[eng].append(fn)

    def finalize(self):
        with self.nc.Block() as block:
            @block.tensor
            def _(e):
                for fn in self.q["tensor"]:
                    fn(e)

            @block.vector
            def _(e):
                for fn in self.q["vector"]:
                    fn(e)

            @block.scalar
            def _(e):
                for fn in self.q["scalar"]:
                    fn(e)

            @block.sync
            def _(e):
                for fn in self.q["sync"]:
                    fn(e)

            @block.gpsimd
            def _(e):
                for fn in self.q["gpsimd"]:
                    fn(e)


def build(cfg: Cfg):
    nc = bass.Bass("TRN2", target_bir_lowering=False)
    D, B, T, F, R, C, L = cfg.D, cfg.B, cfg.T, cfg.F, cfg.R, cfg.C, cfg.L
    MC, DJ, FC = cfg.MC, cfg.DJ, cfg.FC
    KD, KM, KF = cfg.KD, cfg.KM, cfg.KF
    RG, MG, DG, FG = cfg.RG, cfg.MG, cfg.DG, cfg.FG
    NS_M, RT, NT, RT_CH, NS_R = cfg.NS_M, cfg.RT, cfg.NT, cfg.RT_CH, cfg.NS_R
    RTP = RT // P
    M = D
    TWO_N = 2.0 / (B * C * D)
    EPS = 1e-5

    di = lambda n, s, d: nc.dram_tensor(n, s, d, kind="ExternalInput").ap()
    dram = lambda n, s, d: nc.dram_tensor(n, s, d, kind="Internal").ap()

    x_in = di("x", [B, L, D], F32)
    xcol_in = di("xcol", [B, L, DJ], F32)
    w0T_in = di("w0T", [D, MC], F32)
    w1rT_in = di("w1rT", [D, DJ], F32)
    w1c_in = di("w1c", [D, MC], F32)
    b0_in = di("b0c", [MC], F32)
    b1_in = di("b1c", [DJ], F32)
    lrw_in = di("lrw", [D], F32)
    lrb_in = di("lrb", [1], F32)
    fgw_in = di("fgw", [D], F32)
    fgb_in = di("fgb", [1], F32)
    w1f_in = di("w1f", [FC, D], F32)
    b1f_in = di("b1f", [FC], F32)
    w2f_in = di("w2f", [DJ, F], F32)
    b2_in = di("b2c", [DJ], F32)
    idf_in = di("identf", [P, P], F32)
    ones_in = di("ones1", [1, P], F32)
    z_out = nc.dram_tensor("z", [B, L, DJ], F32, kind="ExternalOutput").ap()

    d_xln = [dram(f"d_xln{i}", [R, D], BF16) for i in range(2)]
    d_st = dram("d_st", [4, max(R, 2 * P)], F32)
    d_dhp = dram("d_dhp", [MC, R], BF16)
    d_yT = dram("d_yT", [T, DG, P, R], F32)
    ag_h_in = dram("ag_h_in", [2, MC * R], BF16)
    ag_h_out = dram("ag_h_out", [NC, 2, MC * R], BF16)
    ag_dp_in = dram("ag_dp_in", [DJ, R], BF16)
    ag_dp_out = dram("ag_dp_out", [NC, DJ * R], BF16)
    ag_h2_in = dram("ag_h2_in", [MC, R], BF16)
    ag_h2_out = dram("ag_h2_out", [NC, MC * R], BF16)
    a2a_in = dram("a2a_in", [NC, DJ, MC], BF16)
    a2a_out = dram("a2a_out", [NC, DJ, MC], BF16)
    ar_o_in = dram("ar_o_in", [2, R], F32)
    ar_o_out = dram("ar_o_out", [2, R], F32)
    ar_y_in = dram("ar_y_in", [2, RT], F32)
    ar_y_out = dram("ar_y_out", [2, RT], F32)
    ar_f_in = dram("ar_f_in", [2, RT], F32)
    ar_f_out = dram("ar_f_out", [2, RT], F32)
    ar_s_in = dram("ar_s_in", [2], F32)
    ar_s_out = dram("ar_s_out", [2], F32)
    ag_ly_in = dram("ag_ly_in", [DJ, RT], BF16)
    ag_ly_out = dram("ag_ly_out", [NC, DJ * RT], BF16)
    ag_h1_in = dram("ag_h1_in", [FC, RT], BF16)
    ag_h1_out = dram("ag_h1_out", [NC, FC * RT], BF16)
    d_w1q = dram("d_w1q", [FC, D], BF16)
    d_w1qT = dram("d_w1qT", [KD, P, FC], BF16)
    d_w2q = dram("d_w2q", [DJ, F], BF16)
    d_w2qT = dram("d_w2qT", [KF, P, DJ], BF16)

    stp = ExitStack()
    sbp = lambda n, s, d: stp.enter_context(nc.sbuf_tensor(n, s, d))

    w0T_m = sbp("w0T_m", [P, KD * MC], F32)
    w1rT_m = sbp("w1rT_m", [P, KM * DJ], F32)
    w1c_m = sbp("w1c_m", [P, KD * MC], F32)
    b0_sb = sbp("b0_sb", [P, MG], F32)
    b1_sb = sbp("b1_sb", [P, DG], F32)
    db0_sb = sbp("db0_sb", [P, MG], F32)
    db1_sb = sbp("db1_sb", [P, DG], F32)
    btmp_sb = sbp("btmp_sb", [P, max(MG, DG, 2)], F32)
    bsc_sb = sbp("bsc_sb", [1, 4], F32)
    lrfg_b = sbp("lrfg_b", [P, 2 * KD], BF16)
    scal_sb = sbp("scal_sb", [P, 4], F32)
    idf_sb = sbp("idf_sb", [P, P], F32)
    ones_sb = sbp("ones_sb", [1, P], F32)
    thr_sb = sbp("thr_sb", [P, 4], F32)
    sct = sbp("sct", [1, 16], F32)
    b1f_sb = sbp("b1f_sb", [P, FG], F32)
    b2_sb = sbp("b2_sb", [P, DG], F32)
    stw = sbp("stw", [1, max(2 * R, 2 * RT)], F32)
    st4 = sbp("st4", [1, max(2 * R, 2 * RT)], F32)
    fgm = sbp("fgm", [1, 2 * B + 4], F32)
    scsr = sbp("scsr", [1, R], F32)

    psF = [stp.enter_context(nc.psum_tensor(f"psF{i}", [P, 512], F32)) for i in range(2)]
    psW = [stp.enter_context(nc.psum_tensor(f"psW{i}", [P, 512], F32)) for i in range(4)]
    psM = stp.enter_context(nc.psum_tensor("psM", [P, 1024], F32))

    NLANE = 8
    lanes = [Sem(stp, nc, f"sDL{i}") for i in range(NLANE)]
    lane_rr = [0]
    sC = Sem(stp, nc, "sC")
    sT = Sem(stp, nc, "sT")
    sV = Sem(stp, nc, "sV")
    sS = Sem(stp, nc, "sS")

    bF = [Buf(), Buf()]
    bW = [Buf() for _ in range(4)]
    bM = Buf()
    bXLN = [Buf(), Buf()]
    bGB = Buf()
    bAGHi, bAGHo = Buf(), Buf()
    bAGDPi, bAGDPo = Buf(), Buf()
    bAGH2i, bAGH2o = Buf(), Buf()
    bA2Ai, bA2Ao = Buf(), Buf()
    bARO, bARY, bARF = Buf(), Buf(), Buf()
    bDHP = Buf()
    bSTW = Buf()
    bSCAL = Buf()
    bSCT = Buf()

    p = Prog(nc)

    class Grp:
        __slots__ = ("lane", "fenced")

        def __init__(self):
            self.lane = lanes[lane_rr[0] % NLANE]
            lane_rr[0] += 1
            self.fenced = False

    def dma(out_ap, in_ap, wait=(), transpose=False, slow=False, grp=None):
        g = grp if grp is not None else Grp()
        w = list(wait)
        if not g.fenced:
            w.append((g.lane, g.lane.n))
            g.fenced = True
        p.op("sync", lambda e, o=out_ap, i=in_ap, tr=transpose, sl=slow:
             e.dma_start(out=o, in_=i, transpose=tr, allow_slow_non_contiguous=sl),
             wait=w, inc=[(g.lane, 16)])
        return (g.lane, g.lane.n)

    def all_dma_tags():
        return [(l, l.n) for l in lanes]

    def vop(builder, wait=()):
        p.op("vector", builder, wait=list(wait) + [(sV, sV.n)], inc=[(sV, 1)])
        return (sV, sV.n)

    def sop(builder, wait=()):
        p.op("scalar", builder, wait=list(wait) + [(sS, sS.n)], inc=[(sS, 1)])
        return (sS, sS.n)

    def top(builder, wait=()):
        p.op("tensor", builder, wait=wait, inc=[(sT, 1)])
        return (sT, sT.n)

    def coll(kind, op_, ins, outs, wait):
        if kind == "AllToAll":
            def b(e, i=ins, o=outs):
                with allow_ici_all_to_all():
                    return e.collective_compute(kind, op_, replica_groups=RGRP, ins=i, outs=o)
        else:
            def b(e, i=ins, o=outs):
                return e.collective_compute(kind, op_, replica_groups=RGRP, ins=i, outs=o)
        p.op("gpsimd", b, wait=list(wait) + [(sC, sC.n)], inc=[(sC, 1)])
        return (sC, sC.n)

    pro = ExitStack()
    sbq = lambda n, s, d: pro.enter_context(nc.sbuf_tensor(n, s, d))
    CW = min(D, 2048)
    wtile = sbq("wtile", [P, CW], F32)
    wabs = sbq("wabs", [P, CW], BF16)
    n_t1 = (FC * D) // (P * CW)
    n_t2 = (DJ * F) // (P * CW)
    acc1 = sbq("acc1", [P, n_t1 + n_t2], F32)
    accw = sbq("accw", [1, 2 * P], F32)
    qa = sbq("qa", [P, CW], BF16)
    qb = sbq("qb", [P, CW], BF16)
    ttile = sbq("ttile", [P, max(FC, DJ, 1024)], BF16)
    lrfg_f = sbq("lrfg_f", [P, 2 * KD], F32)

    dma(w0T_m[:, :].rearrange("p (k m) -> p k m", k=KD),
        w0T_in.rearrange("(k p) m -> p k m", p=P))
    dma(w1rT_m[:, :].rearrange("p (k m) -> p k m", k=KM),
        w1rT_in.rearrange("(k p) m -> p k m", p=P))
    dma(w1c_m[:, :].rearrange("p (k m) -> p k m", k=KD),
        w1c_in.rearrange("(k p) m -> p k m", p=P))
    dma(b0_sb[:, :], b0_in.rearrange("(g p) -> p g", p=P), slow=True)
    dma(b1_sb[:, :], b1_in.rearrange("(g p) -> p g", p=P), slow=True)
    dma(b1f_sb[:, :], b1f_in.rearrange("(g p) -> p g", p=P), slow=True)
    dma(b2_sb[:, :], b2_in.rearrange("(g p) -> p g", p=P), slow=True)
    dma(bsc_sb[0:1, 0:1], lrb_in[None, :])
    dma(bsc_sb[0:1, 1:2], fgb_in[None, :])
    dma(idf_sb[:, :], idf_in)
    dma(ones_sb[:, :], ones_in)
    rlw = dma(lrfg_f[:, 0:KD], lrw_in.rearrange("(k p) -> p k", p=P), slow=True)
    rall = dma(lrfg_f[:, KD:2 * KD], fgw_in.rearrange("(k p) -> p k", p=P), slow=True)
    lf_view = lrfg_b[:, :].rearrange("p (k t) -> p t k", t=2)
    vop(lambda e: e.tensor_copy(lf_view[:, 0, :], lrfg_f[:, 0:KD]), wait=[rlw, rall])
    vop(lambda e: e.tensor_copy(lf_view[:, 1, :], lrfg_f[:, KD:2 * KD]))

    w1f_fl = w1f_in.rearrange("a b -> (a b)").rearrange("(t p c) -> t p c", p=P, c=CW)
    w2f_fl = w2f_in.rearrange("a b -> (a b)").rearrange("(t p c) -> t p c", p=P, c=CW)
    for i in range(n_t1 + n_t2):
        src = w1f_fl[i] if i < n_t1 else w2f_fl[i - n_t1]
        rl = dma(wtile[:, :], src, wait=[(sS, sS.n)])
        sop(lambda e, i=i: e.activation(wabs[:, :], wtile[:, :], AF.Abs,
                                        accum_out=acc1[:, i:i + 1]), wait=[rl])
    u1 = vop(lambda e: e.reduce_sum(btmp_sb[:, 0:1], acc1[:, 0:n_t1], axis=AX.X),
             wait=[(sS, sS.n)])
    u2 = vop(lambda e: e.reduce_sum(btmp_sb[:, 1:2], acc1[:, n_t1:n_t1 + n_t2], axis=AX.X))
    r = dma(d_st.rearrange("a b -> (a b)")[0:2 * P].rearrange("(s p) -> p s", p=P),
            btmp_sb[:, 0:2], wait=[u1, u2], slow=True)
    r = dma(accw[:, :], d_st.rearrange("a b -> (a b)")[None, 0:2 * P], wait=[r])
    u1 = vop(lambda e: e.reduce_sum(sct[0:1, 0:1], accw[0:1, 0:P], axis=AX.X), wait=[r])
    u2 = vop(lambda e: e.reduce_sum(sct[0:1, 1:2], accw[0:1, P:2 * P], axis=AX.X))
    r = dma(ar_s_in[None, :], sct[0:1, 0:2], wait=[u1, u2])
    rc = coll("AllReduce", ALU.add, [ar_s_in], [ar_s_out], wait=[r])
    r = dma(sct[0:1, 2:4], ar_s_out[None, :], wait=[rc])
    vop(lambda e: e.tensor_scalar(sct[0:1, 4:6], sct[0:1, 2:4],
                                  1.0 / (F * D), 1e-8, ALU.mult, ALU.max), wait=[r])
    vop(lambda e: e.tensor_scalar(sct[0:1, 6:8], sct[0:1, 4:6], 0.5, None, ALU.mult))
    vop(lambda e: e.tensor_copy(sct[0:1, 8:9], sct[0:1, 6:7]))
    vop(lambda e: e.tensor_scalar(sct[0:1, 9:10], sct[0:1, 6:7], -1.0, None, ALU.mult))
    vop(lambda e: e.tensor_copy(sct[0:1, 10:11], sct[0:1, 7:8]))
    u = vop(lambda e: e.tensor_scalar(sct[0:1, 11:12], sct[0:1, 7:8], -1.0, None, ALU.mult))
    rt = top(lambda e: e.matmul(psM[:, 0:4], ones_sb[0:1, :], sct[0:1, 8:12],
                                start=True, stop=True), wait=[u])
    uth = vop(lambda e: e.tensor_copy(thr_sb[:, :], psM[:, 0:4]), wait=[rt])
    bM.wrote(uth)

    d_w1q_fl = d_w1q.rearrange("a b -> (a b)").rearrange("(t p c) -> t p c", p=P, c=CW)
    d_w2q_fl = d_w2q.rearrange("a b -> (a b)").rearrange("(t p c) -> t p c", p=P, c=CW)
    last_q = None
    for i in range(n_t1 + n_t2):
        src = w1f_fl[i] if i < n_t1 else w2f_fl[i - n_t1]
        dst = d_w1q_fl[i] if i < n_t1 else d_w2q_fl[i - n_t1]
        o = 0 if i < n_t1 else 2
        rl = dma(wtile[:, :], src, wait=([uth] if i == 0 else [(sV, sV.n)]))
        vop(lambda e, o=o: e.tensor_single_scalar(qa[:, :], wtile[:, :],
            thr_sb[:, o:o + 1], ALU.is_gt),
            wait=([rl] + ([last_q] if last_q else [])))
        vop(lambda e, o=o: e.tensor_single_scalar(qb[:, :], wtile[:, :],
            thr_sb[:, o + 1:o + 2], ALU.is_lt))
        uq = vop(lambda e: e.tensor_sub(qa[:, :], qa[:, :], qb[:, :]))
        last_q = dma(dst, qa[:, :], wait=[uq])
    for kd in range(KD):
        rl = dma(ttile[:, 0:FC], d_w1q[:, kd * P:(kd + 1) * P],
                 wait=([last_q] if kd == 0 else all_dma_tags()), transpose=True)
        dma(d_w1qT[kd], ttile[:, 0:FC], wait=[rl])
    for kf in range(KF):
        rl = dma(ttile[:, 0:DJ], d_w2q[:, kf * P:(kf + 1) * P],
                 wait=all_dma_tags(), transpose=True)
        dma(d_w2qT[kf], ttile[:, 0:DJ], wait=[rl])

    pro_done = ([l.n for l in lanes], sV.n, sS.n, sT.n, sC.n)
    pro.close()

    def barrier_tags():
        return ([(l, n) for l, n in zip(lanes, pro_done[0])] +
                [(sV, pro_done[1]), (sS, pro_done[2]),
                 (sT, pro_done[3]), (sC, pro_done[4])])

    pha = ExitStack()
    sba = lambda n, s, d: pha.enter_context(nc.sbuf_tensor(n, s, d))
    w0T_b = sba("w0T_b", [P, KD * MC], BF16)
    w1rT_b = sba("w1rT_b", [P, KM * DJ], BF16)
    w1c_b = sba("w1c_b", [P, KD * MC], BF16)
    gbuf = sba("gbuf", [P, KD * R], BF16)
    xrg = sba("xrg", [P, D], F32)
    xsc = sba("xsc", [P, D], BF16)
    xlnr = sba("xlnr", [P, D], BF16)
    xcolr = sba("xcolr", [P, RG * DJ], F32)
    strg = sba("strg", [P, 4 * RG], F32)
    xT = [sba(f"xT{i}", [P, DG * R], F32) for i in range(2)]
    lnT = [sba(f"lnT{i}", [P, DG * R], F32) for i in range(2)]
    s_sb = sba("s_sb", [P, MG * R], F32)
    hfp = sba("hfp", [P, MG * R], F32)
    fac = sba("fac", [P, MG * R], F32)
    hcT_b = sba("hcT_b", [P, MG * R], BF16)
    hcrm_b = sba("hcrm_b", [P, RG * MC], BF16)
    t_sb = sba("t_sb", [P, DG * R], F32)
    dpT_b = sba("dpT_b", [P, DG * R], BF16)
    dprm_b = sba("dprm_b", [P, RG * DJ], BF16)
    dhpT_b = sba("dhpT_b", [P, MG * R], BF16)
    dhprm_b = sba("dhprm_b", [P, RG * MC], BF16)
    dw1_b = sba("dw1_b", [P, DG * M], BF16)
    w1g_b = sba("w1g_b", [P, KM * DJ], BF16)
    w1cg_b = w1g_b
    tmpf = sba("tmpf", [P, 2048], F32)
    outT_f = sba("outT_f", [P, DG * R], F32)
    yT_f = sba("yT_f", [P, DG * R], F32)
    orm_f = sba("orm_f", [P, RG * DJ], F32)
    stpar = sba("stpar", [P, 2 * RG], F32)

    bXT = [Buf(), Buf()]
    bLNT = [Buf(), Buf()]
    bXC = Buf()

    uc = vop(lambda e: e.tensor_copy(w0T_b[:, :], w0T_m[:, :]), wait=barrier_tags())
    vop(lambda e: e.tensor_copy(w1rT_b[:, :], w1rT_m[:, :]))
    u_wc = vop(lambda e: e.tensor_copy(w1c_b[:, :], w1c_m[:, :]))
    bias_tag = u_wc

    def w0T_tile(k, mg):
        return w0T_b[:, k * MC + mg * P: k * MC + (mg + 1) * P]

    def w1rT_tile(k, mg):
        return w1rT_b[:, k * DJ + mg * P: k * DJ + (mg + 1) * P]

    def w1c_tile(k, mg):
        return w1c_b[:, k * MC + mg * P: k * MC + (mg + 1) * P]

    def gtile(k):
        return gbuf[:, k * R:(k + 1) * R]

    def bcast2(vals_ap_lo, vals_ap_hi, wait):
        w = list(wait) + bM.wguard()
        top(lambda e: e.matmul(psM[:, 0:R], ones_sb[0:1, :], vals_ap_lo,
                               start=True, stop=True), wait=w)
        tagt = top(lambda e: e.matmul(psM[:, 512:512 + R], ones_sb[0:1, :], vals_ap_hi,
                                      start=True, stop=True))
        return tagt

    def emit_prep(t):
        par = t % 2
        gx = Grp()
        wx = None
        for rg in range(RG):
            rl = dma(xrg[:, :], x_in[rg, t * C:(t + 1) * C, :],
                     wait=[(sV, sV.n), (sS, sS.n)])
            u1 = vop(lambda e: e.reduce_sum(strg[:, 0:1], xrg[:, :], axis=AX.X), wait=[rl])
            u2 = sop(lambda e: e.activation(xsc[:, :], xrg[:, :], AF.Square,
                     accum_out=strg[:, RG:RG + 1]), wait=[rl])
            ng = strg[:, 2 * RG + rg:2 * RG + rg + 1]
            rs_ = strg[:, 3 * RG + rg:3 * RG + rg + 1]
            u3 = vop(lambda e, ng=ng: e.tensor_scalar(ng, strg[:, 0:1], -1.0 / D, None,
                     ALU.mult), wait=[u1])
            u4 = vop(lambda e, ng=ng: e.tensor_mul(btmp_sb[:, 0:1], ng, ng))
            u5 = vop(lambda e, rs_=rs_: e.tensor_scalar(rs_, strg[:, RG:RG + 1], 1.0 / D,
                     None, ALU.mult), wait=[u2])
            u6 = vop(lambda e, rs_=rs_: e.tensor_sub(rs_, rs_, btmp_sb[:, 0:1]))
            u7 = vop(lambda e, rs_=rs_: e.tensor_scalar(rs_, rs_, EPS, None, ALU.add))
            u8 = vop(lambda e, rs_=rs_: e.reciprocal(rs_, rs_))
            u9 = sop(lambda e, rs_=rs_: e.activation(rs_, rs_, AF.Sqrt), wait=[u8])
            u10 = vop(lambda e, ng=ng, rs_=rs_: e.tensor_scalar(
                xlnr[:, :], xrg[:, :], ng, rs_, ALU.add, ALU.mult),
                wait=[u9] + ([wx] if wx else []))
            wx = dma(d_xln[par][rg * P:(rg + 1) * P, :], xlnr[:, :],
                     wait=[u10] + (bXLN[par].wguard() if rg == 0 else []), grp=gx)
        bXLN[par].wrote(wx)
        r = dma(d_st[0, 0:R].rearrange("(g p) -> p g", p=P), strg[:, 2 * RG:3 * RG],
                wait=[(sV, sV.n), (sS, sS.n)], slow=True)
        dma(d_st[1, 0:R].rearrange("(g p) -> p g", p=P), strg[:, 3 * RG:4 * RG],
            slow=True)
        rs = dma(stw[0:1, 0:2 * R], d_st.rearrange("a b -> (a b)")[None, 0:2 * R],
                 wait=all_dma_tags() + bSTW.wguard())
        bSTW.wrote(rs)
        rxc = dma(xcolr[:, :].rearrange("p (g j) -> p g j", j=DJ),
                  xcol_in[:, t * C:(t + 1) * C, :].rearrange("b c j -> c b j"),
                  wait=bXC.wguard())
        bXC.wrote(rxc)
        shots = {}
        for dg in range(DG):
            for rg in range(RG):
                wl = [rxc] + (bW[dg].wguard() if rg == 0 else [])
                rr = top(lambda e, dg=dg, rg=rg: e.transpose(
                    psW[dg][:, rg * P:(rg + 1) * P],
                    xcolr[:, rg * DJ + dg * P: rg * DJ + (dg + 1) * P], idf_sb[:, :]),
                    wait=wl)
            shots[dg] = rr
        for dg in range(DG):
            ue = vop(lambda e, dg=dg: e.tensor_copy(xT[par][:, dg * R:(dg + 1) * R],
                     psW[dg][:, 0:R]), wait=[shots[dg]] + bXT[par].wguard())
            bW[dg].wrote(ue)
        bXT[par].wrote((sV, sV.n))
        bXC.used((sV, sV.n))
        tb = bcast2(stw[0:1, 0:R], stw[0:1, R:2 * R], [rs])
        bSTW.used(tb)
        for dg in range(DG):
            vop(lambda e, dg=dg: e.tensor_add(lnT[par][:, dg * R:(dg + 1) * R],
                xT[par][:, dg * R:(dg + 1) * R], psM[:, 0:R]),
                wait=[tb] + (bLNT[par].wguard() if dg == 0 else []))
            vop(lambda e, dg=dg: e.tensor_mul(lnT[par][:, dg * R:(dg + 1) * R],
                lnT[par][:, dg * R:(dg + 1) * R], psM[:, 512:512 + R]))
        bXT[par].used((sV, sV.n))
        bM.wrote((sV, sV.n))
        bLNT[par].wrote((sV, sV.n))

    def stream_chT(par):
        wl = bGB.wguard() + bXLN[par].rguard()
        g = Grp()
        tag = None
        for k in range(KD):
            tag = dma(gtile(k), d_xln[par][:, k * P:(k + 1) * P],
                      wait=(wl if k == 0 else ()), transpose=True, grp=g)
        bXLN[par].used(tag)
        return tag

    def load_gather(blocks_fn, wait):
        GPR = KD // NC
        rload = None
        g = Grp()
        for rb in range(NC):
            rload = dma(gbuf[:, rb * GPR * R:(rb + 1) * GPR * R]
                        .rearrange("p (g r) -> p g r", r=R),
                        blocks_fn(rb), wait=(wait if rb == 0 else ()), grp=g)
        return rload

    ag_h_b0 = ag_h_in[0].rearrange("(m r) -> m r", r=R)
    ag_h_b1 = ag_h_in[1].rearrange("(r m) -> r m", m=MC)

    emit_prep(0)
    for t in range(T):
        par = t % 2
        stream_chT(par)
        wl_mm = (all_dma_tags() + bF[0].wguard() + bF[1].wguard() +
                 bW[2].wguard() + bW[3].wguard())
        first = True
        for k in range(KD):
            for mg in range(MG):
                top(lambda e, k=k, mg=mg: e.matmul(psF[mg][:, 0:R], w0T_tile(k, mg),
                    gtile(k), start=(k == 0), stop=(k == KD - 1)),
                    wait=(wl_mm if first else ()))
                first = False
            top(lambda e, k=k: e.matmul(psW[2][0:1, 0:R], lrfg_b[:, 2 * k:2 * k + 1],
                gtile(k), start=(k == 0), stop=(k == KD - 1)))
            top(lambda e, k=k: e.matmul(psW[3][0:1, 0:R], lrfg_b[:, 2 * k + 1:2 * k + 2],
                gtile(k), start=(k == 0), stop=(k == KD - 1)))
        mm1_done = (sT, sT.n)
        bGB.wrote(mm1_done)
        us_lr = sop(lambda e: e.activation(scsr[0:1, :], psW[2][0:1, 0:R], AF.Sigmoid,
                    bias=bsc_sb[0:1, 0:1], accum_out=fgm[0:1, 2 * B + 1:2 * B + 2]),
                    wait=[mm1_done] + bSCT.rguard())
        u_lr = vop(lambda e: e.tensor_scalar(fgm[0:1, 2 * B + 2:2 * B + 3],
                   fgm[0:1, 2 * B + 1:2 * B + 2], 0.01 / R, None, ALU.mult), wait=[us_lr])
        for b in range(B):
            vop(lambda e, b=b: e.reduce_sum(fgm[0:1, b:b + 1],
                psW[3][0:1, b * C:(b + 1) * C], axis=AX.X),
                wait=([mm1_done] if b == 0 else ()))
        uv = (sV, sV.n)
        us_fg = sop(lambda e: e.activation(fgm[0:1, B:2 * B], fgm[0:1, 0:B], AF.Sigmoid,
                    bias=bsc_sb[0:1, 1:2], scale=1.0 / C), wait=[uv])
        u_fg = vop(lambda e: e.reduce_sum(fgm[0:1, 2 * B + 3:2 * B + 4], fgm[0:1, B:2 * B],
                   axis=AX.X), wait=[us_fg])
        bW[2].wrote(us_lr)
        bW[3].wrote(u_fg)
        u1 = vop(lambda e: e.tensor_scalar(sct[0:1, 0:1], fgm[0:1, 2 * B + 3:2 * B + 4],
                 1.0 / B, None, ALU.mult), wait=[u_fg])
        u2 = vop(lambda e: e.tensor_copy(sct[0:1, 1:2], fgm[0:1, 2 * B + 2:2 * B + 3]))
        bSCT.wrote(u2)
        rb_ = top(lambda e: e.matmul(psM[:, 0:2], ones_sb[0:1, :], sct[0:1, 0:2],
                                     start=True, stop=True), wait=[u1, u2] + bM.wguard())
        bSCT.used(rb_)
        usc = vop(lambda e: e.tensor_copy(scal_sb[:, 0:2], psM[:, 0:2]),
                  wait=[rb_, (sS, sS.n)] + bSCAL.wguard())
        bSCAL.wrote(usc)
        bM.wrote(usc)
        for mg in range(MG):
            u = sop(lambda e, mg=mg: e.activation(s_sb[:, mg * R:(mg + 1) * R],
                    psF[mg][:, 0:R], AF.Sigmoid, bias=b0_sb[:, mg:mg + 1]),
                    wait=[mm1_done, bias_tag])
            u2 = vop(lambda e, mg=mg: e.scalar_tensor_tensor(
                hfp[:, mg * R:(mg + 1) * R], psF[mg][:, 0:R], b0_sb[:, mg:mg + 1],
                s_sb[:, mg * R:(mg + 1) * R], ALU.add, ALU.mult), wait=[u])
            bF[mg].wrote(u2)
        uh = vop(lambda e: e.tensor_copy(hcT_b[:, :], hfp[:, :]))
        vop(lambda e: e.tensor_scalar(fac[:, :], s_sb[:, :], -1.0, 1.0, ALU.mult, ALU.add))
        vop(lambda e: e.scalar_tensor_tensor(fac[:, :], hfp[:, :], 1.0, fac[:, :],
                                             ALU.mult, ALU.mult))
        ufac = vop(lambda e: e.tensor_add(fac[:, :], fac[:, :], s_sb[:, :]))
        wh = dma(ag_h_b0.rearrange("(g p) r -> p g r", p=P),
                 hcT_b[:, :].rearrange("p (g r) -> p g r", r=R),
                 wait=[uh] + bAGHi.wguard())
        g = Grp()
        for rg in range(RG):
            trs = dma(hcrm_b[:, rg * MC:(rg + 1) * MC], ag_h_b0[:, rg * P:(rg + 1) * P],
                      wait=([wh] if rg == 0 else ()), transpose=True, grp=g)
        wh2 = dma(ag_h_b1.rearrange("(g p) m -> p g m", p=P),
                  hcrm_b[:, :].rearrange("p (g m) -> p g m", m=MC), wait=[trs])
        cag_h = coll("AllGather", ALU.bypass, [ag_h_in.rearrange("a b -> (a b)")],
                     [ag_h_out.rearrange("a b c -> (a b c)")],
                     wait=[wh2] + bAGHo.wguard())
        bAGHi.wrote(wh2)
        bAGHi.used(cag_h)
        bAGHo.wrote(cag_h)
        if t + 1 < T:
            emit_prep(t + 1)
        rload = load_gather(lambda rb: ag_h_out[rb, 0]
                            .rearrange("(g p r) -> p g r", p=P, r=R),
                            bGB.wguard() + [cag_h])
        bAGHo.used(rload)
        wl = [rload] + bF[0].wguard() + bF[1].wguard()
        first = True
        for k in range(KM):
            for mg in range(DG):
                top(lambda e, k=k, mg=mg: e.matmul(psF[mg][:, 0:R], w1rT_tile(k, mg),
                    gtile(k), start=(k == 0), stop=(k == KM - 1)),
                    wait=(wl if first else ()))
                first = False
        mm2_done = (sT, sT.n)
        bGB.wrote(mm2_done)
        for mg in range(DG):
            vop(lambda e, mg=mg, lt=lnT[par]: e.scalar_tensor_tensor(
                t_sb[:, mg * R:(mg + 1) * R], psF[mg][:, 0:R], b1_sb[:, mg:mg + 1],
                lt[:, mg * R:(mg + 1) * R], ALU.add, ALU.subtract),
                wait=[mm2_done, bias_tag] + ([bLNT[par].ready] if mg == 0 else []))
            u2 = vop(lambda e, mg=mg: e.tensor_scalar(
                dpT_b[:, mg * R:(mg + 1) * R], t_sb[:, mg * R:(mg + 1) * R],
                TWO_N, None, ALU.mult, op1=ALU.add,
                accum_out=db1_sb[:, mg:mg + 1]))
            bF[mg].wrote(u2)
        bLNT[par].used((sV, sV.n))
        udp = (sV, sV.n)
        wdp = dma(ag_dp_in.rearrange("(g p) r -> p g r", p=P),
                  dpT_b[:, :].rearrange("p (g r) -> p g r", r=R),
                  wait=[udp] + bAGDPi.wguard())
        g = Grp()
        for rg in range(RG):
            rdprm = dma(dprm_b[:, rg * DJ:(rg + 1) * DJ], ag_dp_in[:, rg * P:(rg + 1) * P],
                        wait=([wdp] if rg == 0 else ()), transpose=True, grp=g)
        cag_dp = coll("AllGather", ALU.bypass,
                      [ag_dp_in.rearrange("a b -> (a b)")],
                      [ag_dp_out.rearrange("a b -> (a b)")],
                      wait=[wdp] + bAGDPo.wguard())
        bAGDPi.wrote(wdp)
        bAGDPi.used(cag_dp)
        bAGDPi.used(rdprm)
        bAGDPo.wrote(cag_dp)
        rload = load_gather(lambda rb: ag_dp_out[rb]
                            .rearrange("(g p r) -> p g r", p=P, r=R),
                            bGB.wguard() + [cag_dp])
        bAGDPo.used(rload)
        wl = [rload] + bF[0].wguard() + bF[1].wguard()
        first = True
        for k in range(KD):
            for mg in range(MG):
                top(lambda e, k=k, mg=mg: e.matmul(psF[mg][:, 0:R], w1c_tile(k, mg),
                    gtile(k), start=(k == 0), stop=(k == KD - 1)),
                    wait=(wl if first else ()))
                first = False
        dh_done = (sT, sT.n)
        bGB.wrote(dh_done)
        for mg in range(MG):
            u = vop(lambda e, mg=mg: e.scalar_tensor_tensor(
                dhpT_b[:, mg * R:(mg + 1) * R], psF[mg][:, 0:R], 1.0,
                fac[:, mg * R:(mg + 1) * R], ALU.mult, ALU.mult,
                accum_out=db0_sb[:, mg:mg + 1]), wait=[dh_done, ufac])
            bF[mg].wrote(u)
        udh = (sV, sV.n)
        wdh = dma(d_dhp.rearrange("(g p) r -> p g r", p=P),
                  dhpT_b[:, :].rearrange("p (g r) -> p g r", r=R),
                  wait=[udh] + bDHP.wguard())
        g = Grp()
        for rg in range(RG):
            rdhprm = dma(dhprm_b[:, rg * MC:(rg + 1) * MC], d_dhp[:, rg * P:(rg + 1) * P],
                         wait=([wdh] if rg == 0 else ()), transpose=True, grp=g)
        bDHP.wrote(wdh)
        bDHP.used(rdhprm)
        RPS = 512 // MC

        def hrm_rhs(kr, ns):
            return ag_h_out[:, 1].rearrange("n (r m) -> n r m", m=MC)[
                ns * RPS:(ns + 1) * RPS, kr * P:(kr + 1) * P, :].rearrange("n p m -> p n m")
        rload = None
        wl0 = bGB.wguard() + [rdprm, cag_h]
        g = Grp()
        for ns in range(NS_M):
            for kr in range(RG):
                rload = dma(gbuf[:, (ns * RG + kr) * 512:(ns * RG + kr + 1) * 512]
                            .rearrange("p (n m) -> p n m", m=MC),
                            hrm_rhs(kr, ns),
                            wait=(wl0 if (ns == 0 and kr == 0) else ()), grp=g)
        bAGHo.used(rload)
        udw1 = None
        for mg in range(DG):
            wl = [rload] + [g for i in range(NS_M) for g in bW[i].wguard()]
            first = True
            for kr in range(RG):
                for ns in range(NS_M):
                    top(lambda e, mg=mg, kr=kr, ns=ns: e.matmul(
                        psW[ns][:, 0:512],
                        dprm_b[:, kr * DJ + mg * P: kr * DJ + (mg + 1) * P],
                        gbuf[:, (ns * RG + kr) * 512:(ns * RG + kr + 1) * 512],
                        start=(kr == 0), stop=(kr == RG - 1)),
                        wait=(wl if first else ()))
                    first = False
            mmw = (sT, sT.n)
            for ns in range(NS_M):
                u = sop(lambda e, mg=mg, ns=ns: e.activation(
                    dw1_b[:, mg * M + ns * 512: mg * M + (ns + 1) * 512],
                    psW[ns][:, 0:512], AF.Copy), wait=[mmw])
                bW[ns].wrote(u)
        udw1 = (sS, sS.n)
        bGB.wrote((sT, sT.n))
        ga2 = Grp()
        a2av = a2a_in.rearrange("n (g p) m -> p g n m", p=P)
        dw1v = dw1_b[:, :].rearrange("p (g n m) -> p g n m", n=NC, m=MC)
        wa = None
        for gdj in range(DG):
            wa = dma(a2av[:, gdj], dw1v[:, gdj],
                     wait=(([udw1] + bA2Ai.wguard()) if gdj == 0 else ()), grp=ga2)
        ca2a = coll("AllToAll", ALU.bypass, [a2a_in.rearrange("a b c -> (a b c)")],
                    [a2a_out.rearrange("a b c -> (a b c)")],
                    wait=[wa] + bA2Ao.wguard())
        bA2Ai.wrote(wa)
        bA2Ai.used(ca2a)
        bA2Ao.wrote(ca2a)
        g = Grp()
        for k in range(KM):
            blk, off = k // DG, (k % DG) * P
            rw1g = dma(w1g_b[:, k * DJ:(k + 1) * DJ], a2a_in[blk, :, off:off + P],
                       wait=([wa] if k == 0 else ()), transpose=True, grp=g)
        bA2Ai.used(rw1g)
        rload = None
        wl0 = bGB.wguard() + [rdhprm] + bXLN[par].rguard()
        g = Grp()
        for rg in range(RG):
            rload = dma(gbuf[:, rg * D:(rg + 1) * D], d_xln[par][rg * P:(rg + 1) * P, :],
                        wait=(wl0 if rg == 0 else ()), grp=g)
        bXLN[par].used(rload)
        NWAVE = (KD + 3) // 4
        for w in range(NWAVE):
            kds = list(range(w * 4, min((w + 1) * 4, KD)))
            wl = [rload] + [g for i in range(4) for g in bW[i].wguard()]
            first = True
            for kd in kds:
                for kr in range(RG):
                    top(lambda e, kd=kd, kr=kr: e.matmul(
                        psW[kd % 4][:, 0:MC],
                        gbuf[:, kr * D + kd * P: kr * D + (kd + 1) * P],
                        dhprm_b[:, kr * MC:(kr + 1) * MC],
                        start=(kr == 0), stop=(kr == RG - 1)),
                        wait=(wl if first else ()))
                    first = False
            mmw = (sT, sT.n)
            for kd in kds:
                u = sop(lambda e, kd=kd: e.activation(
                    tmpf[:, (kd % 4) * MC:(kd % 4 + 1) * MC], psW[kd % 4][:, 0:MC],
                    AF.Identity, scale=scal_sb[:, 1:2]),
                    wait=[mmw, usc] + ([(sV, sV.n)] if kd == kds[0] else []))
                bW[kd % 4].wrote(u)
            us = (sS, sS.n)
            for kd in kds:
                vop(lambda e, kd=kd: e.scalar_tensor_tensor(
                    w0T_m[:, kd * MC:(kd + 1) * MC], w0T_m[:, kd * MC:(kd + 1) * MC],
                    scal_sb[:, 0:1], tmpf[:, (kd % 4) * MC:(kd % 4 + 1) * MC],
                    ALU.mult, ALU.subtract),
                    wait=([us, usc, (sT, sT.n)] if kd == kds[0] else [us]))
                vop(lambda e, kd=kd: e.tensor_copy(
                    w0T_b[:, kd * MC:(kd + 1) * MC], w0T_m[:, kd * MC:(kd + 1) * MC]))
        bGB.wrote((sT, sT.n))
        bSCAL.used((sV, sV.n))
        bSCAL.used((sS, sS.n))
        WU = min(2048, KM * DJ)
        for h in range((KM * DJ) // WU):
            vop(lambda e, h=h: e.tensor_scalar(tmpf[:, 0:WU],
                w1g_b[:, h * WU:(h + 1) * WU], scal_sb[:, 1:2], None, ALU.mult),
                wait=([rw1g, usc, (sT, sT.n), (sS, sS.n)] if h == 0 else ()))
            vop(lambda e, h=h: e.scalar_tensor_tensor(
                w1rT_m[:, h * WU:(h + 1) * WU], w1rT_m[:, h * WU:(h + 1) * WU],
                scal_sb[:, 0:1], tmpf[:, 0:WU], ALU.mult, ALU.subtract))
            vop(lambda e, h=h: e.tensor_copy(w1rT_b[:, h * WU:(h + 1) * WU],
                w1rT_m[:, h * WU:(h + 1) * WU]))
        rload = dma(w1cg_b[:, :].rearrange("p (n h m) -> p n h m", h=DG, m=MC),
                    a2a_out.rearrange("n (h p) m -> p n h m", p=P),
                    wait=[ca2a, (sV, sV.n)])
        bA2Ao.used(rload)
        WU2 = min(2048, KD * MC)
        for h in range((KD * MC) // WU2):
            vop(lambda e, h=h: e.tensor_scalar(tmpf[:, 0:WU2],
                w1cg_b[:, h * WU2:(h + 1) * WU2], scal_sb[:, 1:2], None, ALU.mult),
                wait=([rload, (sT, sT.n)] if h == 0 else ()))
            vop(lambda e, h=h: e.scalar_tensor_tensor(
                w1c_m[:, h * WU2:(h + 1) * WU2], w1c_m[:, h * WU2:(h + 1) * WU2],
                scal_sb[:, 0:1], tmpf[:, 0:WU2], ALU.mult, ALU.subtract))
            vop(lambda e, h=h: e.tensor_copy(w1c_b[:, h * WU2:(h + 1) * WU2],
                w1c_m[:, h * WU2:(h + 1) * WU2]))
        vop(lambda e: e.tensor_single_scalar(b0_sb[:, :], b0_sb[:, :],
            scal_sb[:, 0:1], ALU.mult), wait=[(sS, sS.n)])
        vop(lambda e: e.tensor_single_scalar(btmp_sb[:, 0:MG], db0_sb[:, :],
            scal_sb[:, 1:2], ALU.mult))
        vop(lambda e: e.tensor_sub(b0_sb[:, :], b0_sb[:, :], btmp_sb[:, 0:MG]))
        vop(lambda e: e.tensor_single_scalar(b1_sb[:, :], b1_sb[:, :],
            scal_sb[:, 0:1], ALU.mult))
        vop(lambda e: e.tensor_single_scalar(btmp_sb[:, 0:DG], db1_sb[:, :],
            scal_sb[:, 1:2], ALU.mult))
        vop(lambda e: e.tensor_sub(b1_sb[:, :], b1_sb[:, :], btmp_sb[:, 0:DG]))
        bias_tag = (sV, sV.n)
        upd_done = (sV, sV.n)
        stream_chT(par)
        wl = all_dma_tags() + [upd_done] + bF[0].wguard() + bF[1].wguard()
        first = True
        for k in range(KD):
            for mg in range(MG):
                top(lambda e, k=k, mg=mg: e.matmul(psF[mg][:, 0:R], w0T_tile(k, mg),
                    gtile(k), start=(k == 0), stop=(k == KD - 1)),
                    wait=(wl if first else ()))
                first = False
        mm6_done = (sT, sT.n)
        bGB.wrote(mm6_done)
        for mg in range(MG):
            u = sop(lambda e, mg=mg: e.activation(s_sb[:, mg * R:(mg + 1) * R],
                    psF[mg][:, 0:R], AF.Sigmoid, bias=b0_sb[:, mg:mg + 1]),
                    wait=[mm6_done, bias_tag])
            u2 = vop(lambda e, mg=mg: e.scalar_tensor_tensor(
                hfp[:, mg * R:(mg + 1) * R], psF[mg][:, 0:R], b0_sb[:, mg:mg + 1],
                s_sb[:, mg * R:(mg + 1) * R], ALU.add, ALU.mult), wait=[u])
            bF[mg].wrote(u2)
        uh2 = vop(lambda e: e.tensor_copy(hcT_b[:, :], hfp[:, :]))
        wh2b = dma(ag_h2_in.rearrange("(g p) r -> p g r", p=P),
                   hcT_b[:, :].rearrange("p (g r) -> p g r", r=R),
                   wait=[uh2] + bAGH2i.wguard())
        cag_h2 = coll("AllGather", ALU.bypass,
                      [ag_h2_in.rearrange("a b -> (a b)")],
                      [ag_h2_out.rearrange("a b -> (a b)")],
                      wait=[wh2b] + bAGH2o.wguard())
        bAGH2i.wrote(wh2b)
        bAGH2i.used(cag_h2)
        bAGH2o.wrote(cag_h2)
        rload = load_gather(lambda rb: ag_h2_out[rb]
                            .rearrange("(g p r) -> p g r", p=P, r=R),
                            bGB.wguard() + [cag_h2])
        bAGH2o.used(rload)
        wl = [rload] + bF[0].wguard() + bF[1].wguard()
        first = True
        for k in range(KM):
            for mg in range(DG):
                top(lambda e, k=k, mg=mg: e.matmul(psF[mg][:, 0:R], w1rT_tile(k, mg),
                    gtile(k), start=(k == 0), stop=(k == KM - 1)),
                    wait=(wl if first else ()))
                first = False
        mm7_done = (sT, sT.n)
        bGB.wrote(mm7_done)
        for mg in range(DG):
            u = sop(lambda e, mg=mg: e.activation(outT_f[:, mg * R:(mg + 1) * R],
                    psF[mg][:, 0:R], AF.Identity, bias=b1_sb[:, mg:mg + 1]),
                    wait=[mm7_done, bias_tag])
            bF[mg].wrote(u)
        uout = (sS, sS.n)
        shots = {}
        for rg in range(RG):
            for dg in range(DG):
                wl = [uout] + (bW[rg % 4].wguard() if dg == 0 else [])
                rr = top(lambda e, rg=rg, dg=dg: e.transpose(
                    psW[rg % 4][:, dg * P:(dg + 1) * P],
                    outT_f[:, dg * R + rg * P: dg * R + (rg + 1) * P], idf_sb[:, :]),
                    wait=wl)
            shots[rg] = rr
        for rg in range(RG):
            u = vop(lambda e, rg=rg: e.tensor_copy(orm_f[:, rg * DJ:(rg + 1) * DJ],
                    psW[rg % 4][:, 0:DJ]),
                    wait=[shots[rg]] + ([(sS, sS.n)] if rg == 0 else []))
            bW[rg % 4].wrote(u)
            vop(lambda e, rg=rg: e.reduce_sum(stpar[:, rg:rg + 1],
                orm_f[:, rg * DJ:(rg + 1) * DJ], axis=AX.X))
            sop(lambda e, rg=rg: e.activation(xsc[:, 0:DJ], orm_f[:, rg * DJ:(rg + 1) * DJ],
                AF.Square, accum_out=stpar[:, RG + rg:RG + rg + 1]), wait=[(sV, sV.n)])
        wst = dma(ar_o_in.rearrange("s (g p) -> p s g", p=P),
                  stpar[:, :].rearrange("p (s g) -> p s g", g=RG),
                  wait=[(sV, sV.n), (sS, sS.n)] + bARO.wguard(), slow=True)
        car = coll("AllReduce", ALU.add, [ar_o_in.rearrange("a b -> (a b)")],
                   [ar_o_out.rearrange("a b -> (a b)")], wait=[wst])
        bARO.wrote(car)
        rst = dma(st4[0:1, 0:2 * R], ar_o_out.rearrange("a b -> (a b)")[None, :], wait=[car])
        bARO.used(rst)
        vop(lambda e: e.tensor_scalar(stw[0:1, 0:R], st4[0:1, 0:R], -1.0 / D, None,
            ALU.mult), wait=[rst] + bSTW.wguard())
        vop(lambda e: e.tensor_mul(st4[0:1, 0:R], stw[0:1, 0:R], stw[0:1, 0:R]))
        vop(lambda e: e.tensor_scalar(stw[0:1, R:2 * R], st4[0:1, R:2 * R], 1.0 / D,
            None, ALU.mult))
        vop(lambda e: e.tensor_sub(stw[0:1, R:2 * R], stw[0:1, R:2 * R], st4[0:1, 0:R]))
        vop(lambda e: e.tensor_scalar(stw[0:1, R:2 * R], stw[0:1, R:2 * R], EPS, None,
            ALU.add))
        uu = vop(lambda e: e.reciprocal(stw[0:1, R:2 * R], stw[0:1, R:2 * R]))
        uu = sop(lambda e: e.activation(stw[0:1, R:2 * R], stw[0:1, R:2 * R], AF.Sqrt),
                 wait=[uu])
        bSTW.wrote(uu)
        tb = bcast2(stw[0:1, 0:R], stw[0:1, R:2 * R], [uu, (sV, sV.n)])
        bSTW.used(tb)
        for dg in range(DG):
            vop(lambda e, dg=dg: e.tensor_add(yT_f[:, dg * R:(dg + 1) * R],
                outT_f[:, dg * R:(dg + 1) * R], psM[:, 0:R]), wait=[tb])
            vop(lambda e, dg=dg: e.tensor_mul(yT_f[:, dg * R:(dg + 1) * R],
                yT_f[:, dg * R:(dg + 1) * R], psM[:, 512:512 + R]))
            vop(lambda e, dg=dg, xp=xT[par]: e.tensor_add(yT_f[:, dg * R:(dg + 1) * R],
                yT_f[:, dg * R:(dg + 1) * R], xp[:, dg * R:(dg + 1) * R]),
                wait=([bXT[par].ready] if dg == 0 else ()))
        bXT[par].used((sV, sV.n))
        bM.wrote((sV, sV.n))
        dma(d_yT[t].rearrange("g p r -> p g r"),
            yT_f[:, :].rearrange("p (g r) -> p g r", r=R), wait=[(sV, sV.n)])
    phaseA_done = ([l.n for l in lanes], sV.n, sS.n, sT.n, sC.n)
    pha.close()

    phb = ExitStack()
    sbb = lambda n, s, d: phb.enter_context(nc.sbuf_tensor(n, s, d))
    w1qT_sb = sbb("w1qT_sb", [P, KD * FC], BF16)
    w2s = [sbb(f"w2s{i}", [P, FG * DJ], BF16) for i in range(2)]
    lny_sb = sbb("lny_sb", [P, KD * RT], BF16)
    _h1 = sbb("h1blk0", [P, FG * RT], BF16)
    h1blk = [_h1, _h1]
    yTt = sbb("yTt", [P, DG * RT], F32)
    o2T = sbb("o2T", [P, DG * RT], F32)
    lyT_b = sbb("lyT_b", [P, DG * RT], BF16)
    h1t_b = sbb("h1t_b", [P, RT], BF16)
    yrm = sbb("yrm", [P, RTP * DJ], F32)
    sqsc = sbb("sqsc", [P, DJ], BF16)
    stpb = sbb("stpb", [P, 2 * RTP], F32)
    stwb = stw
    st4b = st4
    zT = sbb("zT", [P, DG * RT], F32)
    zrm = yrm

    bar = ([(l, n) for l, n in zip(lanes, phaseA_done[0])] +
           [(sV, phaseA_done[1]), (sS, phaseA_done[2]),
            (sT, phaseA_done[3]), (sC, phaseA_done[4])])
    _bh1 = Buf()
    bH1 = [_bh1, _bh1]
    bZ = Buf()
    bAGLYi, bAGLYo = Buf(), Buf()
    bAGH1i, bAGH1o = Buf(), Buf()
    bH1T = Buf()
    bLNY = Buf()

    rw1 = dma(w1qT_sb[:, :].rearrange("p (k f) -> p k f", k=KD),
              d_w1qT.rearrange("k p f -> p k f"), wait=bar)
    bW2S = [Buf(), Buf()]

    def ln_fm(src_f32, out_fp32, out_bf16, add_src, stats_in, stats_out, b_ar, waits):
        shots = {}
        for rt in range(RTP):
            for dg in range(DG):
                wl = list(waits) + (bW[rt % 4].wguard() if dg == 0 else [])
                rr = top(lambda e, rt=rt, dg=dg: e.transpose(
                    psW[rt % 4][:, dg * P:(dg + 1) * P],
                    src_f32[:, dg * RT + rt * P: dg * RT + (rt + 1) * P], idf_sb[:, :]),
                    wait=wl)
            shots[rt] = rr
            u = vop(lambda e, rt=rt: e.tensor_copy(yrm[:, rt * DJ:(rt + 1) * DJ],
                    psW[rt % 4][:, 0:DJ]),
                    wait=[rr] + (([(sS, sS.n)] + bZ.wguard()) if rt == 0 else []))
            bW[rt % 4].wrote(u)
            vop(lambda e, rt=rt: e.reduce_sum(stpb[:, rt:rt + 1],
                yrm[:, rt * DJ:(rt + 1) * DJ], axis=AX.X))
            sop(lambda e, rt=rt: e.activation(sqsc[:, :], yrm[:, rt * DJ:(rt + 1) * DJ],
                AF.Square, accum_out=stpb[:, RTP + rt:RTP + rt + 1]), wait=[(sV, sV.n)])
        wst = dma(stats_in.rearrange("s (g p) -> p s g", p=P),
                  stpb[:, :].rearrange("p (s g) -> p s g", g=RTP),
                  wait=[(sV, sV.n), (sS, sS.n)] + b_ar.wguard(), slow=True)
        car = coll("AllReduce", ALU.add, [stats_in.rearrange("a b -> (a b)")],
                   [stats_out.rearrange("a b -> (a b)")], wait=[wst])
        b_ar.wrote(car)
        rst = dma(st4b[0:1, :], stats_out.rearrange("a b -> (a b)")[None, :], wait=[car])
        b_ar.used(rst)
        vop(lambda e: e.tensor_scalar(stwb[0:1, 0:RT], st4b[0:1, 0:RT], -1.0 / D,
            None, ALU.mult), wait=[rst])
        vop(lambda e: e.tensor_mul(st4b[0:1, 0:RT], stwb[0:1, 0:RT], stwb[0:1, 0:RT]))
        vop(lambda e: e.tensor_scalar(stwb[0:1, RT:], st4b[0:1, RT:], 1.0 / D, None,
            ALU.mult))
        vop(lambda e: e.tensor_sub(stwb[0:1, RT:], stwb[0:1, RT:], st4b[0:1, 0:RT]))
        vop(lambda e: e.tensor_scalar(stwb[0:1, RT:], stwb[0:1, RT:], EPS, None, ALU.add))
        uu = vop(lambda e: e.reciprocal(stwb[0:1, RT:], stwb[0:1, RT:]))
        uu = sop(lambda e: e.activation(stwb[0:1, RT:], stwb[0:1, RT:], AF.Sqrt), wait=[uu])
        last = None
        for hf in range(RT // 512):
            tb = top(lambda e, hf=hf: e.matmul(psM[:, 0:512], ones_sb[0:1, :],
                     stwb[0:1, hf * 512:(hf + 1) * 512], start=True, stop=True),
                     wait=[uu, (sV, sV.n)] + bM.wguard())
            tb = top(lambda e, hf=hf: e.matmul(psM[:, 512:1024], ones_sb[0:1, :],
                     stwb[0:1, RT + hf * 512: RT + (hf + 1) * 512], start=True, stop=True))
            for dg in range(DG):
                off = dg * RT + hf * 512
                dst = out_fp32 if out_fp32 is not None else zT
                vop(lambda e, off=off, dst=dst: e.tensor_add(dst[:, off:off + 512],
                    src_f32[:, off:off + 512], psM[:, 0:512]), wait=[tb])
                last = vop(lambda e, off=off, dst=dst: e.tensor_mul(dst[:, off:off + 512],
                           dst[:, off:off + 512], psM[:, 512:1024]))
                if out_bf16 is not None:
                    last = vop(lambda e, off=off, dst=dst: e.tensor_copy(
                        out_bf16[:, off:off + 512], dst[:, off:off + 512]))
                if add_src is not None:
                    last = vop(lambda e, off=off, dst=dst: e.tensor_add(
                        dst[:, off:off + 512], dst[:, off:off + 512],
                        add_src[:, off:off + 512]))
            bM.wrote((sV, sV.n))
        return last

    for tt in range(NT):
        ry = None
        gy = Grp()
        yv = yTt[:, :].rearrange("p (g c r) -> p g c r", g=DG, c=RT_CH)
        for ch in range(RT_CH):
            t = tt * RT_CH + ch
            ry = dma(yv[:, :, ch, :], d_yT[t].rearrange("g p r -> p g r"),
                     wait=(bar if (tt == 0 and ch == 0) else
                           (bLNY.wguard() if ch == 0 else ())), grp=gy)
        uln = ln_fm(yTt, None, lyT_b, None, ar_y_in, ar_y_out, bARY, [ry])
        bLNY.wrote(uln)
        wly = dma(ag_ly_in.rearrange("(g p) r -> p g r", p=P),
                  lyT_b[:, :].rearrange("p (g r) -> p g r", r=RT),
                  wait=[uln] + bAGLYi.wguard())
        cly = coll("AllGather", ALU.bypass, [ag_ly_in.rearrange("a b -> (a b)")],
                   [ag_ly_out.rearrange("a b -> (a b)")], wait=[wly] + bAGLYo.wguard())
        bAGLYi.wrote(wly)
        bAGLYi.used(cly)
        bAGLYo.wrote(cly)
        rload = None
        GPR = KD // NC
        gl = Grp()
        for rb in range(NC):
            rload = dma(lny_sb[:, rb * GPR * RT:(rb + 1) * GPR * RT]
                        .rearrange("p (g r) -> p g r", r=RT),
                        ag_ly_out[rb].rearrange("(g p r) -> p g r", p=P, r=RT),
                        wait=([cly, (sT, sT.n)] if rb == 0 else ()), grp=gl)
        bAGLYo.used(rload)
        wh1 = None
        for fg in range(FG):
            wl = [rload, rw1] + bF[0].wguard() + bF[1].wguard()
            first = True
            for k in range(KD):
                for ns in range(NS_R):
                    top(lambda e, fg=fg, k=k, ns=ns: e.matmul(
                        psF[ns][:, 0:512],
                        w1qT_sb[:, k * FC + fg * P: k * FC + (fg + 1) * P],
                        lny_sb[:, k * RT + ns * 512: k * RT + (ns + 1) * 512],
                        start=(k == 0), stop=(k == KD - 1)),
                        wait=(wl if first else ()))
                    first = False
            mmd = (sT, sT.n)
            for ns in range(NS_R):
                u = sop(lambda e, fg=fg, ns=ns: e.activation(
                    h1t_b[:, ns * 512:(ns + 1) * 512], psF[ns][:, 0:512], AF.Gelu,
                    bias=b1f_sb[:, fg:fg + 1]), wait=[mmd] + bH1T.wguard())
                bF[ns].wrote(u)
            bH1T.wrote((sS, sS.n))
            wh1 = dma(ag_h1_in[fg * P:(fg + 1) * P, :], h1t_b[:, :],
                      wait=[(sS, sS.n)] + (bAGH1i.wguard() if fg == 0 else []))
            bH1T.used(wh1)
        ch1 = coll("AllGather", ALU.bypass,
                   [ag_h1_in.rearrange("a b -> (a b)")],
                   [ag_h1_out.rearrange("a b -> (a b)")],
                   wait=[wh1] + bAGH1o.wguard())
        bAGH1i.wrote(wh1)
        bAGH1i.used(ch1)
        bAGH1o.wrote(ch1)
        first_mm = True
        mmd = None
        for rb in range(NC):
            rl = dma(h1blk[rb % 2][:, :].rearrange("p (g r) -> p g r", r=RT),
                     ag_h1_out[rb].rearrange("(g p r) -> p g r", p=P, r=RT),
                     wait=([ch1] + bH1[rb % 2].wguard()) if rb < 2 else bH1[rb % 2].wguard())
            rl2 = dma(w2s[rb % 2][:, :].rearrange("p (k j) -> p k j", k=FG),
                      d_w2qT[rb * FG:(rb + 1) * FG].rearrange("k p j -> p k j"),
                      wait=bW2S[rb % 2].wguard())
            for kk in range(FG):
                k = rb * FG + kk
                for mg in range(DG):
                    for ns in range(NS_R):
                        if first_mm:
                            wl = [rl, rl2] + [g for i in range(4) for g in bW[i].wguard()]
                        elif kk == 0 and mg == 0 and ns == 0:
                            wl = [rl, rl2]
                        else:
                            wl = ()
                        top(lambda e, k=k, kk=kk, rb=rb, mg=mg, ns=ns: e.matmul(
                            psW[mg * NS_R + ns][:, 0:512],
                            w2s[rb % 2][:, kk * DJ + mg * P: kk * DJ + (mg + 1) * P],
                            h1blk[rb % 2][:, kk * RT + ns * 512: kk * RT + (ns + 1) * 512],
                            start=(k == 0), stop=(k == KF - 1)),
                            wait=wl)
                        first_mm = False
            mmd = (sT, sT.n)
            bH1[rb % 2].wrote(mmd)
            bW2S[rb % 2].wrote(mmd)
        for mg in range(DG):
            for ns in range(NS_R):
                u = sop(lambda e, mg=mg, ns=ns: e.activation(
                    o2T[:, mg * RT + ns * 512: mg * RT + (ns + 1) * 512],
                    psW[mg * NS_R + ns][:, 0:512], AF.Identity,
                    bias=b2_sb[:, mg:mg + 1]), wait=[mmd])
                bW[mg * NS_R + ns].wrote(u)
        uo2 = (sS, sS.n)
        uz = ln_fm(o2T, zT, None, yTt, ar_f_in, ar_f_out, bARF, [uo2])
        bLNY.used(uz)
        for rt in range(RTP):
            for dg in range(DG):
                wl = ([uz] + bW[rt % 4].wguard()) if dg == 0 else ()
                rr = top(lambda e, rt=rt, dg=dg: e.transpose(
                    psW[rt % 4][:, dg * P:(dg + 1) * P],
                    zT[:, dg * RT + rt * P: dg * RT + (rt + 1) * P], idf_sb[:, :]),
                    wait=wl)
            u = vop(lambda e, rt=rt: e.tensor_copy(zrm[:, rt * DJ:(rt + 1) * DJ],
                    psW[rt % 4][:, 0:DJ]),
                    wait=[rr] + (bZ.wguard() if rt == 0 else []))
            bW[rt % 4].wrote(u)
        uzr = (sV, sV.n)
        gz = Grp()
        wz = None
        for ch in range(RT_CH):
            t = tt * RT_CH + ch
            for b in range(B):
                rt = ch * B + b
                wz = dma(z_out[b, t * C:(t + 1) * C, :],
                         zrm[:, rt * DJ:(rt + 1) * DJ],
                         wait=([uzr] if (ch == 0 and b == 0) else ()), grp=gz)
        bZ.wrote(uzr)
        bZ.used(wz)
    p.op("sync", lambda e: e.nop(), wait=all_dma_tags() + [(sC, sC.n), (sT, sT.n),
                                          (sV, sV.n), (sS, sS.n)])
    p.finalize()
    phb.close()
    stp.close()
    return nc


def _shard_inputs(cfg, inp):
    MC, DJ, FC = cfg.MC, cfg.DJ, cfg.FC
    x = np.ascontiguousarray(np.asarray(inp["L0_attn_x"], dtype=np.float32))
    W0 = np.asarray(inp["L0_attn_W0"], np.float32)
    b0 = np.asarray(inp["L0_attn_b0"], np.float32)
    W1 = np.asarray(inp["L0_attn_W1"], np.float32)
    b1 = np.asarray(inp["L0_attn_b1"], np.float32)
    lrw = np.asarray(inp["L0_attn_lr_w"], np.float32).reshape(-1)
    lrb = np.asarray(inp["L0_attn_lr_b"], np.float32).reshape(-1)
    fgw = np.asarray(inp["L0_attn_fg_w"], np.float32).reshape(-1)
    fgb = np.asarray(inp["L0_attn_fg_b"], np.float32).reshape(-1)
    fw1 = np.asarray(inp["L0_ffn_w1"], np.float32)
    fb1 = np.asarray(inp["L0_ffn_b1"], np.float32)
    fw2 = np.asarray(inp["L0_ffn_w2"], np.float32)
    fb2 = np.asarray(inp["L0_ffn_b2"], np.float32)
    ident = np.eye(P, dtype=np.float32)
    ones1 = np.ones((1, P), np.float32)
    maps = []
    for c in range(NC):
        mc = slice(c * MC, (c + 1) * MC)
        dj = slice(c * DJ, (c + 1) * DJ)
        fc = slice(c * FC, (c + 1) * FC)
        maps.append({
            "x": x,
            "xcol": np.ascontiguousarray(x[:, :, dj]),
            "w0T": np.ascontiguousarray(W0[mc, :].T),
            "w1rT": np.ascontiguousarray(W1[dj, :].T),
            "w1c": np.ascontiguousarray(W1[:, mc]),
            "b0c": np.ascontiguousarray(b0[mc]),
            "b1c": np.ascontiguousarray(b1[dj]),
            "lrw": lrw, "lrb": lrb, "fgw": fgw, "fgb": fgb,
            "w1f": np.ascontiguousarray(fw1[fc, :]),
            "b1f": np.ascontiguousarray(fb1[fc]),
            "w2f": np.ascontiguousarray(fw2[dj, :]),
            "b2c": np.ascontiguousarray(fb2[dj]),
            "identf": ident, "ones1": ones1,
        })
    return maps


def kernel(**inputs):
    cfg = Cfg()
    maps = _shard_inputs(cfg, inputs)
    nc = build(cfg)
    res = run_bass_kernel_spmd(nc, maps, core_ids=list(range(NC)))
    z = np.concatenate([np.asarray(res.results[c]["z"]) for c in range(NC)], axis=2)
    return z.astype(np.float32)
